# version 06 (final)
# baseline (speedup 1.0000x reference)
import contextlib
import numpy as np
import ml_dtypes
import concourse.bass as bass
import concourse.mybir as mybir
from concourse.bass_utils import run_bass_kernel_spmd

F32 = mybir.dt.float32
BF16 = mybir.dt.bfloat16
ALU = mybir.AluOpType
AF = mybir.ActivationFunctionType
AX = mybir.AxisListType.X

N = 1024
TT = 512
TS = 4096
NTILE = TS // TT
NEXP = 9
FI = 938
FIP = 944
GWID = 8 * FIP
WDEV_W = 2 * GWID + 8 * 1024
PIECE_OFF = [0, 4 * FIP, GWID, GWID + 4 * FIP, 2 * GWID, 2 * GWID + 4096]
PIECE_W = [4 * FIP, 4 * FIP, 4 * FIP, 4 * FIP, 4096, 4096]
ENGINES = ("gpsimd", "scalar", "tensor", "vector", "sync")


def _host_constants():
    j = np.arange(N)
    T = np.zeros((N, N), dtype=np.float64)
    T[0] = 1.0
    T[512] = (-1.0) ** j
    for p in range(1, 512):
        T[p] = np.cos(2 * np.pi * p * j / N)
        T[512 + p] = -np.sin(2 * np.pi * p * j / N)
    Ti = np.zeros((N, N), dtype=np.float64)
    Ti[:, 0] = 1.0 / N
    Ti[:, 512] = ((-1.0) ** j) / N
    for p in range(1, 512):
        Ti[:, p] = (2.0 / N) * np.cos(2 * np.pi * p * j / N)
        Ti[:, 512 + p] = -(2.0 / N) * np.sin(2 * np.pi * p * j / N)

    def hilo(M):
        hi = M.astype(np.float32)
        lo = (M - hi.astype(np.float64)).astype(np.float32)
        return np.ascontiguousarray(hi), np.ascontiguousarray(lo)

    tmt_hi, tmt_lo = hilo(T.T)
    tinv_hi, tinv_lo = hilo(Ti)
    pp = np.arange(512)[:, None]
    jj = np.arange(511)[None, :] + 1
    Sm = np.zeros((512, 512), dtype=np.float64)
    Sm[:, :511] = -2.0 * np.sin(2 * np.pi * pp * jj / N)
    Sm[0, :] = 0.0
    smv_hi, smv_lo = hilo(Sm)
    return dict(tmt_hi=tmt_hi, tmt_lo=tmt_lo, tinv_hi=tinv_hi, tinv_lo=tinv_lo,
                smv_hi=smv_hi, smv_lo=smv_lo,
                ident=np.eye(128, dtype=ml_dtypes.bfloat16),
                i8=np.eye(8, dtype=ml_dtypes.bfloat16))


class Prog:

    def __init__(self, nc):
        self.nc = nc
        self.ops = []
        self.handles = {}
        self.cnt = {}
        self.producer = {}
        self._lastw = {}
        self.ppr = 0
        self.last_vec = None

    def mark_ppr(self):
        self.ppr = self.at("cV")[1]

    def sem(self, stack, name, producer=None):
        self.handles[name] = stack.enter_context(self.nc.semaphore(name))
        self.cnt[name] = 0
        self.producer[name] = producer

    def at(self, name):
        if name == "cV":
            rec = self.last_vec
            if rec is not None and not rec["flag"]:
                rec["flag"] = True
                self.cnt["cV"] += 1
        return (name, self.cnt[name])

    def emit(self, engine, fn, waits=(), inc=None, inc_n=1, drain=False,
             dinc=False):
        wl = []
        flat = []
        for w in waits:
            if isinstance(w, list):
                flat.extend(w)
            else:
                flat.append(w)
        for w in flat:
            if w is None:
                continue
            s, v = w if isinstance(w, tuple) else (w, self.cnt[w])
            if isinstance(v, tuple):
                s, v = v
            if v <= 0 or self.producer.get(s) == engine:
                continue
            wl.append((s, v))
        if engine == "tensor" and inc is not None:
            dinc = True

        rec = {"flag": False}
        lazy = (engine == "vector" and inc == "cV" and fn is not None)
        if inc is not None and not lazy:
            self.cnt[inc] += inc_n
            rec["flag"] = True
        handles = self.handles
        lastw = self._lastw

        def do(eng, fn=fn, wl=wl, inc=inc, inc_n=inc_n, engine=engine,
               drain=drain, dinc=dinc, rec=rec):
            for s, v in wl:
                key = (engine, s)
                if lastw.get(key, 0) < v:
                    eng.wait_ge(handles[s], v)
                    lastw[key] = v
            if drain:
                eng.drain()
            if fn is not None:
                instr = fn(eng)
                if inc is not None and rec["flag"]:
                    if dinc:
                        eng.drain().then_inc(handles[inc], inc_n)
                    else:
                        instr.then_inc(handles[inc], inc_n)

        self.ops.append((engine, do))
        if lazy:
            self.last_vec = rec
            return None
        return self.cnt[inc] if inc is not None else None

    def flush(self):
        if not self.ops:
            return
        ops, self.ops = self.ops, []
        with self.nc.Block() as blk:
            for name in ENGINES:
                mine = [fn for (e, fn) in ops if e == name]
                if mine:
                    getattr(blk, name)(lambda eng, mine=mine: [f(eng) for f in mine])

    def snapshot(self, names=("cT", "cV", "cS", "sSy", "sSc", "sGp", "sOut")):
        return [(s, self.cnt[s]) for s in names]

    def fence(self, marks):
        for eng in ENGINES:
            self.emit(eng, None, waits=marks)


def build_program(dbg=False, lite=False, serc=False, sera=False, serf='', serp=0, serw=False):
    nc = bass.Bass("TRN2", target_bir_lowering=False)

    u4 = nc.dram_tensor("u4", [TS, N], F32, kind="ExternalInput").ap()
    ap_bc = nc.dram_tensor("ap_bc", [128, 512], F32, kind="ExternalInput").ap()
    bw = nc.dram_tensor("bw", [N, N], F32, kind="ExternalInput").ap()
    cwm = nc.dram_tensor("cwm", [N, N], F32, kind="ExternalInput").ap()
    dv_bc = nc.dram_tensor("dv_bc", [128, N], F32, kind="ExternalInput").ap()
    gwt_in = nc.dram_tensor("gwt_in", [8, N], F32, kind="ExternalInput").ap()
    sgu = nc.dram_tensor("sgu", [2 * FI, N], F32, kind="ExternalInput").ap()
    sdn = nc.dram_tensor("sdn", [N, FI], F32, kind="ExternalInput").ap()
    egu = nc.dram_tensor("egu", [8, 2 * FI, N], F32, kind="ExternalInput").ap()
    edn = nc.dram_tensor("edn", [8, N, FI], F32, kind="ExternalInput").ap()
    tmt_hi = nc.dram_tensor("tmt_hi", [N, N], F32, kind="ExternalInput").ap()
    tmt_lo = nc.dram_tensor("tmt_lo", [N, N], F32, kind="ExternalInput").ap()
    tinv_hi = nc.dram_tensor("tinv_hi", [N, N], F32, kind="ExternalInput").ap()
    tinv_lo = nc.dram_tensor("tinv_lo", [N, N], F32, kind="ExternalInput").ap()
    smv_hi = nc.dram_tensor("smv_hi", [512, 512], F32, kind="ExternalInput").ap()
    smv_lo = nc.dram_tensor("smv_lo", [512, 512], F32, kind="ExternalInput").ap()
    ident_i = nc.dram_tensor("ident", [128, 128], BF16, kind="ExternalInput").ap()
    i8_i = nc.dram_tensor("i8", [8, 8], BF16, kind="ExternalInput").ap()
    sel_bc = nc.dram_tensor("sel_bc", [128, 12], F32, kind="ExternalInput").ap()
    out4 = nc.dram_tensor("out4", [TS, N], F32, kind="ExternalOutput").ap()

    gcast = nc.dram_tensor("gcast", [NEXP, FIP, N], BF16, kind="Internal").ap()
    ucast = nc.dram_tensor("ucast", [NEXP, FIP, N], BF16, kind="Internal").ap()
    dcast = nc.dram_tensor("dcast", [NEXP, N, N], BF16, kind="Internal").ap()
    wdev = nc.dram_tensor("wdev", [NEXP, 128, WDEV_W], BF16, kind="Internal").ap()
    xnt_scr = nc.dram_tensor("xnt_scr", [NTILE, 128, 8 * TT], BF16, kind="Internal").ap()
    ub_scr = nc.dram_tensor("ub_scr", [NTILE, 128, 4 * N], F32, kind="Internal").ap()
    x_scr = nc.dram_tensor("x_scr", [NTILE, 4, 128, N], F32, kind="Internal").ap()
    cc_in = nc.dram_tensor("cc_in", [128, 8], F32, kind="Internal").ap()
    cc_out = nc.dram_tensor("cc_out", [4, 128, 8], F32, kind="Internal").ap()

    dbg_out = {}
    if dbg:
        for nm, shp, dt in [("dbg_bdev", [128, 8192], BF16),
                            ("dbg_cdev", [128, 8192], BF16),
                            ("dbg_tabc", [128, 2048], F32),
                            ("dbg_tabs", [128, 2048], F32),
                            ("dbg_lam", [128, 8], F32),
                            ("dbg_l512", [128, 8], F32),
                            ("dbg_icol", [128, 8], F32),
                            ("dbg_xf", [128, 4096], F32),
                            ("dbg_xnt", [128, 4096], BF16),
                            ("dbg_gwt", [128, 64], BF16),
                            ("dbg_xn", [128, 4096], BF16),
                            ("dbg_xn2", [128, 4096], BF16),
                            ("dbg_cols", [128, 16], F32),
                            ("dbg_wdev", [2, 128, WDEV_W], BF16),
                            ("dbg_ghat", [128, 4096], BF16),
                            ("dbg_zt", [128, 4096], BF16),
                            ("dbg_yacc", [128, 4096], F32),
                            ("dbg_xscr3", [4, 128, 1024], F32),
                            ("dbg_ubscr3", [128, 4096], F32),
                            ("dbg_xntscr3", [128, 4096], BF16),
                            ("dbg_ub", [128, 4096], F32),
                            ("dbg_cw", [128, 256], F32)]:
            dbg_out[nm] = nc.dram_tensor(nm, shp, dt, kind="ExternalOutput").ap()

    p = Prog(nc)
    st = contextlib.ExitStack()
    with st:
        for nm, prod in [("sGp", None), ("sSy", None), ("sSc", None),
                         ("sOut", None), ("sCC", None),
                         ("cT", "tensor"), ("cV", "vector"), ("cS", "scalar")]:
            p.sem(st, nm, prod)

        def SBg(name, shape, dt):
            return st.enter_context(nc.sbuf_tensor(name, shape, dt))

        def dma(engine, out, in_, sem="sSy", waits=(), transpose=False):
            if transpose:
                fn = lambda e: e.dma_start(out=out, in_=in_, transpose=True)
            else:
                fn = lambda e: e.dma_start(out=out, in_=in_)
            return p.emit(engine, fn, waits=waits, inc=sem, inc_n=16)

        lam = SBg("lam", [128, 8], F32)
        l512 = SBg("l512", [128, 8], F32)
        pdl = SBg("pdl", [128, 24], F32)
        tabc = SBg("tabc", [128, 2048], F32)
        tabs = SBg("tabs", [128, 2048], F32)
        ones5 = SBg("ones5", [128, TT], F32)
        gwt = SBg("gwt", [128, 64], BF16)
        cwall = SBg("cwall", [128, 256], F32)
        icol = SBg("icol", [128, 8], F32)
        ccol = [SBg(f"ccol{i}", [128, 8], F32) for i in range(2)]
        ccbuf = SBg("ccbuf", [128, 8], F32)
        colA = SBg("colA", [128, 256], F32)
        zztile = SBg("zztile", [128, N], BF16)
        idt = SBg("idt", [128, 128], BF16)
        i8t = SBg("i8t", [8, 8], BF16)
        cdev = SBg("cdev", [128, 8192], BF16)
        dbc = SBg("dbc", [128, N], F32)
        dbg_cols_sb = SBg("dbg_cols_sb", [128, 16], F32)
        selb = SBg("selb", [128, 12], F32)
        sBD = contextlib.ExitStack()
        bdev = sBD.enter_context(nc.sbuf_tensor("bdev", [128, 8192], BF16))

        def ca(i, w=1):
            return colA[:, i:i + w]

        V = lambda fn, waits=(), : p.emit("vector", fn, waits=waits, inc="cV")
        Vflush = lambda: p.emit("vector", lambda e: e.drain(), inc="cV")
        Sflush = lambda: p.emit("scalar", lambda e: e.drain(), inc="cS")
        Vd = lambda fn, waits=(), : p.emit("vector", fn, waits=waits, inc="cV", drain=True)
        S = lambda fn, waits=(): p.emit("scalar", fn, waits=waits, inc="cS")
        Sd = lambda fn, waits=(): p.emit("scalar", fn, waits=waits, inc="cS", dinc=True)

        def cmul_cols(dre, dim, are, aim, bre, bim):
            t0, t1 = ca(60), ca(61)
            Vd(lambda e: e.tensor_mul(t0, are, bre))
            Vd(lambda e: e.tensor_mul(t1, aim, bim))
            Vd(lambda e: e.tensor_sub(ca(62), t0, t1))
            Vd(lambda e: e.tensor_mul(t0, aim, bre))
            Vd(lambda e: e.tensor_mul(t1, are, bim))
            Vd(lambda e: e.tensor_add(dim, t0, t1))
            Vd(lambda e: e.tensor_copy(dre, ca(62)))

        s0 = contextlib.ExitStack()
        with s0:
            SB0 = lambda name, shape, dt: s0.enter_context(nc.sbuf_tensor(name, shape, dt))
            smh = SB0("smh", [128, 2048], F32)
            sml = SB0("sml", [128, 2048], F32)
            abc = SB0("abc", [128, 512], F32)
            gwb = SB0("gwb", [8, N], BF16)
            w512a = SB0("w512a", [128, 512], F32)
            w512b = SB0("w512b", [128, 512], F32)
            ps_g = s0.enter_context(nc.psum_tensor("ps_g", [128, 64], F32))

            V(lambda e: e.memset(zztile[:, :], 0.0))
            V(lambda e: e.memset(ones5[:, :], 1.0))
            zz_ready = p.at("cV")

            cast_done = {}
            for e0 in range(NEXP):
                gu_src = sgu if e0 == 0 else egu[e0 - 1]
                dn_src = sdn if e0 == 0 else edn[e0 - 1]
                dma("gpsimd", gcast[e0, 0:FI, :], gu_src[0:FI, :], "sGp")
                dma("gpsimd", gcast[e0, FI:FIP, :], zztile[0:6, :], "sGp", waits=[zz_ready])
                dma("gpsimd", ucast[e0, 0:FI, :], gu_src[FI:2 * FI, :], "sGp")
                dma("gpsimd", ucast[e0, FI:FIP, :], zztile[0:6, :], "sGp")
                dma("gpsimd", dcast[e0, :, 0:FI], dn_src[:, :], "sGp")
                dma("gpsimd",
                    dcast[e0, :, FI:N].rearrange("(a b) c -> b a c", b=128),
                    zztile[:, 0:8 * (N - FI)].rearrange("b (a c) -> b a c", a=8),
                    "sGp")
                cast_done[e0] = p.at("sGp")

            dma("sync", smh[:, :].rearrange("p (a j) -> p a j", a=4),
                smv_hi[:, :].rearrange("(a p) j -> p a j", p=128))
            dma("sync", sml[:, :].rearrange("p (a j) -> p a j", a=4),
                smv_lo[:, :].rearrange("(a p) j -> p a j", p=128))
            dma("sync", abc[:, :], ap_bc[:, :])
            dma("sync", idt[:, :], ident_i[:, :])
            dma("sync", i8t[:, :], i8_i[:, :])
            dma("sync", dbc[:, :], dv_bc[:, :])
            dma("sync", selb[:, :], sel_bc[:, :])
            small_ld = p.at("sSy")
            dma("gpsimd", gwb[:, :], gwt_in[:, :], "sGp")
            gwb_ld = p.at("sGp")

            for dc in range(8):
                p.emit("tensor",
                       lambda e, dc=dc: e.matmul(ps_g[:, dc * 8:(dc + 1) * 8],
                                                 gwb[0:8, dc * 128:(dc + 1) * 128],
                                                 i8t[:, :], start=True, stop=True),
                       waits=[small_ld, gwb_ld] if dc == 0 else [],
                       inc="cT" if dc == 7 else None)
            gw_mm = p.at("cT")
            V(lambda e: e.tensor_copy(gwt[:, :], ps_g[:, :]), waits=[("cT", gw_mm)])
            p.mark_ppr()
            gwt_ready = p.at("cV")

            for pc in range(4):
                sl = slice(pc * 512, (pc + 1) * 512)
                om = ca(0 + pc)
                V(lambda e, sl=sl: e.tensor_mul(w512a[:, :], smh[:, sl], abc[:, :]),
                  waits=[small_ld])
                for q_ in range(16):
                    V(lambda e, q_=q_: e.reduce_sum(ca(40 + q_), w512a[:, q_ * 32:(q_ + 1) * 32], axis=AX))
                V(lambda e, sl=sl: e.tensor_mul(w512b[:, :], sml[:, sl], abc[:, :]))
                V(lambda e: e.reduce_sum(ca(8), w512b[:, :], axis=AX))
                for st_ in (8, 4, 2, 1):
                    for q_ in range(st_):
                        Vd(lambda e, q_=q_, st_=st_: e.tensor_add(ca(40 + q_), ca(40 + q_), ca(40 + q_ + st_)))
                Vd(lambda e, om=om: e.tensor_add(om, ca(40), ca(8)))
                w2, rec = ca(9), ca(10)
                Vd(lambda e, om=om, w2=w2: e.tensor_mul(w2, om, om))
                Vd(lambda e, w2=w2: e.tensor_scalar_add(ca(11), w2, 1.0))
                Vd(lambda e, rec=rec: e.reciprocal(rec, ca(11)))
                Vd(lambda e, rec=rec: e.tensor_mul(ca(14), ca(11), rec))
                Vd(lambda e: e.tensor_scalar(ca(15), ca(14), 2.0, -1.0, ALU.subtract, ALU.mult))
                Vd(lambda e, rec=rec: e.tensor_mul(rec, rec, ca(15)))
                Vd(lambda e, w2=w2: e.tensor_scalar(ca(12), w2, 1.0, -1.0, ALU.subtract, ALU.mult))
                Vd(lambda e, pc=pc, rec=rec: e.tensor_mul(lam[:, pc:pc + 1], ca(12), rec))
                Vd(lambda e, om=om, rec=rec: e.tensor_mul(ca(13), om, rec))
                Vd(lambda e, pc=pc: e.tensor_scalar_mul(lam[:, 4 + pc:5 + pc], ca(13), -2.0))

            pwre, pwim, tre, tim = ca(16), ca(17), ca(18), ca(19)
            for pc in range(4):
                c0 = pc * 512
                Vd(lambda e, c0=c0: e.memset(tabc[:, c0:c0 + 1], 1.0))
                Vd(lambda e, c0=c0: e.memset(tabs[:, c0:c0 + 1], 0.0))
                Vd(lambda e, pc=pc: e.tensor_copy(pwre, lam[:, pc:pc + 1]))
                Vd(lambda e, pc=pc: e.tensor_copy(pwim, lam[:, 4 + pc:5 + pc]))
                size = 1
                while size < 512:
                    cs = slice(c0, c0 + size)
                    cd = slice(c0 + size, c0 + 2 * size)
                    w = size
                    Vd(lambda e, cs=cs, w=w: e.tensor_scalar(w512a[:, 0:w], tabc[:, cs], pwre, None, ALU.mult))
                    Vd(lambda e, cs=cs, w=w: e.tensor_scalar(w512b[:, 0:w], tabs[:, cs], pwim, None, ALU.mult))
                    Vd(lambda e, cd=cd, w=w: e.tensor_sub(tabc[:, cd], w512a[:, 0:w], w512b[:, 0:w]))
                    Vd(lambda e, cs=cs, w=w: e.tensor_scalar(w512a[:, 0:w], tabs[:, cs], pwre, None, ALU.mult))
                    Vd(lambda e, cs=cs, w=w: e.tensor_scalar(w512b[:, 0:w], tabc[:, cs], pwim, None, ALU.mult))
                    Vd(lambda e, cd=cd, w=w: e.tensor_add(tabs[:, cd], w512a[:, 0:w], w512b[:, 0:w]))
                    cmul_cols(pwre, pwim, pwre, pwim, pwre, pwim)
                    size *= 2
                Vd(lambda e, pc=pc: e.tensor_copy(l512[:, pc:pc + 1], pwre))
                Vd(lambda e, pc=pc: e.tensor_copy(l512[:, 4 + pc:5 + pc], pwim))
                for _ in range(3):
                    cmul_cols(pwre, pwim, pwre, pwim, pwre, pwim)
                Vd(lambda e, pc=pc: e.tensor_copy(pdl[:, pc:pc + 1], l512[:, pc:pc + 1]))
                Vd(lambda e, pc=pc: e.tensor_copy(pdl[:, 4 + pc:5 + pc], l512[:, 4 + pc:5 + pc]))
                for d in (1, 2):
                    cmul_cols(pdl[:, d * 8 + pc:d * 8 + pc + 1],
                              pdl[:, d * 8 + 4 + pc:d * 8 + 5 + pc],
                              pdl[:, (d - 1) * 8 + pc:(d - 1) * 8 + pc + 1],
                              pdl[:, (d - 1) * 8 + 4 + pc:(d - 1) * 8 + 5 + pc],
                              pwre, pwim)
            tables_ready = p.at("cV")
            p.flush()

        s1 = contextlib.ExitStack()
        with s1:
            p.fence(p.snapshot(("cT", "cV", "cS")))
            SB1 = lambda name, shape, dt: s1.enter_context(nc.sbuf_tensor(name, shape, dt))
            bigf = SB1("bigf", [128, 2048], F32)
            lo32 = SB1("lo32", [128, 2048], F32)
            m_hi = SB1("m_hi", [128, 8192], BF16)
            m_lo = SB1("m_lo", [128, 8192], BF16)
            t_hi = SB1("t_hi", [128, 8192], BF16)
            t_lo = SB1("t_lo", [128, 8192], BF16)
            ps_tx = [s1.enter_context(nc.psum_tensor(f"ps_tx{i}", [128, 512], F32))
                     for i in range(2)]
            ps_guard = [None, None]

            def split_matrix(src32, srclo32, dst_hi, dst_lo, guard=()):
                for cb in range(4):
                    rs, re = cb * 256, (cb + 1) * 256
                    dsl_h = dst_hi[:, cb * 2048:(cb + 1) * 2048]
                    dsl_l = dst_lo[:, cb * 2048:(cb + 1) * 2048]
                    dma("sync", bigf[:, :].rearrange("p (a d) -> p a d", a=2),
                        src32[rs:re, :].rearrange("(a p) d -> p a d", p=128),
                        waits=list(guard) + [p.at("cV")] if cb == 0 else
                        [p.at("cV")])
                    ld1 = p.at("sSy")
                    ld2 = ld1
                    if srclo32 is not None:
                        dma("sync", lo32[:, :].rearrange("p (a d) -> p a d", a=2),
                            srclo32[rs:re, :].rearrange("(a p) d -> p a d", p=128))
                        ld2 = p.at("sSy")
                    V(lambda e, dsl_h=dsl_h: e.tensor_copy(dsl_h, bigf[:, :]),
                      waits=[("sSy", ld1)] + list(guard))
                    V(lambda e, dsl_h=dsl_h: e.tensor_sub(bigf[:, :], bigf[:, :], dsl_h))
                    if srclo32 is not None:
                        V(lambda e: e.tensor_add(bigf[:, :], bigf[:, :], lo32[:, :]),
                          waits=[("sSy", ld2)])
                    V(lambda e, dsl_l=dsl_l: e.tensor_copy(dsl_l, bigf[:, :]))
                return p.at("cV")

            def transform(dst, lhs_hi, lhs_lo, rhs_hi, rhs_lo, readies):
                for oc in range(8):
                    for fh in range(2):
                        if serp >= 2:
                            p.fence(p.snapshot())
                        bank = (oc * 2 + fh) % 2
                        ps = ps_tx[bank]
                        passes = [(lhs_hi, rhs_hi), (lhs_hi, rhs_lo), (lhs_lo, rhs_hi)]
                        for pi, (Am, Bm) in enumerate(passes):
                            for kc in range(8):
                                lhs = Am[:, kc * 1024 + oc * 128: kc * 1024 + (oc + 1) * 128]
                                rhs = Bm[:, kc * 1024 + fh * 512: kc * 1024 + (fh + 1) * 512]
                                first = (pi == 0 and kc == 0)
                                last = (pi == 2 and kc == 7)
                                p.emit("tensor",
                                       lambda e, ps=ps, lhs=lhs, rhs=rhs, first=first, last=last:
                                       e.matmul(ps[:, :], lhs, rhs, start=first, stop=last),
                                       waits=(list(readies) + [ps_guard[bank]]) if first else [],
                                       inc="cT" if last else None)
                        mmv = p.at("cT")
                        dsl = dst[:, oc * 1024 + fh * 512: oc * 1024 + (fh + 1) * 512]
                        V(lambda e, dsl=dsl, ps=ps: e.tensor_copy(dsl, ps[:, :]),
                          waits=[("cT", mmv)])
                        p.mark_ppr()
                        ps_guard[bank] = p.at("cV")
                transform.last_mm = ("cT", p.cnt["cT"])
                return p.at("cV")

            mb = split_matrix(bw, None, m_hi, m_lo)
            tb = split_matrix(tmt_hi, tmt_lo, t_hi, t_lo)
            bdev_ready = transform(bdev, m_hi, m_lo, t_hi, t_lo, [("cV", tb)])
            bdev_mm = transform.last_mm

            cm = split_matrix(cwm, None, m_hi, m_lo, guard=[bdev_mm])
            ct_hi = SB1("ct_hi", [128, 8192], BF16)
            ct_lo = SB1("ct_lo", [128, 8192], BF16)
            ps_tb = s1.enter_context(nc.psum_tensor("ps_tb", [128, 512], BF16))
            ps_tb_guard = None
            for (srcm, dstm) in ((m_hi, ct_hi), (m_lo, ct_lo)):
                for jc in range(8):
                    for half in range(2):
                        for q in range(4):
                            dc = half * 4 + q
                            sap = srcm[:, dc * 1024 + jc * 128: dc * 1024 + (jc + 1) * 128]
                            dap = ps_tb[:, q * 128:(q + 1) * 128]
                            p.emit("tensor",
                                   lambda e, sap=sap, dap=dap: e.transpose(dap, sap, idt[:, :]),
                                   waits=([("cV", cm), small_ld, ps_tb_guard]
                                          if q == 0 else []),
                                   inc="cT" if q == 3 else None)
                        tmv = p.at("cT")
                        dsl = dstm[:, jc * 1024 + half * 512: jc * 1024 + (half + 1) * 512]
                        V(lambda e, dsl=dsl: e.tensor_copy(dsl, ps_tb[:, :]),
                          waits=[("cT", tmv)])
                        p.mark_ppr()
                        ps_tb_guard = p.at("cV")
            ctld = p.at("cV")
            if serp >= 1:
                p.fence(p.snapshot())
            tib = split_matrix(tinv_hi, tinv_lo, t_hi, t_lo, guard=[bdev_mm])
            if serp >= 1:
                p.fence(p.snapshot())
            cdev_ready = transform(cdev, t_hi, t_lo, ct_hi, ct_lo,
                                   [("cV", tib), ctld])
            p.flush()

        fenceA = p.snapshot(("cT", "cV", "cS", "sSy", "sSc"))
        ttmp_guard = [None, None, None, None]
        wdev_done = {}
        _kk = [0]

        def emit_pret(e0):
            for kind in range(3):
                for c in range(8):
                    par = _kk[0] % 4
                    _kk[0] += 1
                    if kind < 2:
                        src = (gcast if kind == 0 else ucast)[e0, 0:FIP, c * 128:(c + 1) * 128]
                        dst_off = kind * GWID + c * FIP
                        wid = FIP
                    else:
                        src = dcast[e0, 0:N, c * 128:(c + 1) * 128]
                        dst_off = 2 * GWID + c * 1024
                        wid = 1024
                    dma("scalar", ttmp[par][:, 0:wid], src, "sSc",
                        waits=[cast_done[e0], ttmp_guard[par]], transpose=True)
                    tr = p.at("sSc")
                    dma("scalar", wdev[e0, :, dst_off:dst_off + wid],
                        ttmp[par][:, 0:wid], "sSc", waits=[("sSc", tr)])
                    ttmp_guard[par] = p.at("sSc")
            wdev_done[e0] = p.at("sSc")

        sA = contextlib.ExitStack()
        xf = None
        with sA:
            SBa = lambda name, shape, dt: sA.enter_context(nc.sbuf_tensor(name, shape, dt))
            uf = [SBa(f"uf{i}", [128, 4 * N], F32) for i in range(2)]
            ttmp = [SBa(f"ttmp{i}", [128, 1024], BF16) for i in range(4)]
            xn = SBa("xn", [128, 4 * N], BF16)
            xnt = SBa("xnt", [128, 8 * TT], BF16)
            ubase = SBa("ubase", [128, 4 * N], F32)
            bp = SBa("bp", [128, 2 * TT], F32)
            tmpa = SBa("tmpa", [128, TT], F32)
            tmpb = SBa("tmpb", [128, TT], F32)
            sq32 = SBa("sq32", [128, N], F32)
            xf = [SBa(f"xf{i}", [128, 4096], F32) for i in range(2)]
            agt = [SBa(f"agt{r}", [128, 8], F32) for r in range(4)]
            ps_t = [sA.enter_context(nc.psum_tensor(f"ps_t{i}", [128, TT], BF16)) for i in range(2)]
            ps_b = [sA.enter_context(nc.psum_tensor(f"ps_b{i}", [128, TT], F32)) for i in range(2)]
            ps_l = sA.enter_context(nc.psum_tensor("ps_l", [128, 32], F32))
            ps_t_guard = [None, None]
            ps_b_guard = [None, None]
            ps_l_guard = None
            xn_guard = None
            xnt_guard = None
            ubase_guard = None
            xb_guard = [None, None]
            uld = [None, None]
            uf_guard = [None, None]

            p.fence(fenceA)
            V(lambda e: e.memset(icol[:, :], 0.0))
            epsc, halfc = ca(64), ca(65)
            V(lambda e: e.memset(epsc, 1e-5))
            V(lambda e: e.memset(halfc, -0.5))
            ln_consts = p.at("cV")

            dma("sync", uf[0][:, :].rearrange("p (a d) -> p a d", a=4),
                u4[0:TT, :].rearrange("(a q) d -> q a d", q=128))
            uld[0] = p.at("sSy")

            for m in range(NTILE):
                if sera:
                    p.fence(p.snapshot(("cT", "cV", "cS", "sSy", "sSc")))
                if not lite:
                    emit_pret(m)
                par = m % 2
                if m + 1 < NTILE:
                    nxt = (m + 1) % 2
                    dma("sync", uf[nxt][:, :].rearrange("p (a d) -> p a d", a=4),
                        u4[(m + 1) * TT:(m + 2) * TT, :].rearrange("(a q) d -> q a d", q=128),
                        waits=[uf_guard[nxt]])
                    uld[nxt] = p.at("sSy")

                for tc in range(4):
                    usl = uf[par][:, tc * N:(tc + 1) * N]
                    V(lambda e, usl=usl, tc=tc: e.reduce_sum(ca(24 + tc), usl, axis=AX),
                      waits=[("sSy", uld[par]), xn_guard if tc == 0 else None])
                    V(lambda e, usl=usl: e.tensor_mul(sq32[:, :], usl, usl))
                    V(lambda e, tc=tc: e.reduce_sum(ca(28 + tc), sq32[:, :], axis=AX))
                Vd(lambda e: e.tensor_scalar_mul(ca(32, 4), ca(24, 4), 1.0 / N))
                Vd(lambda e: e.tensor_scalar_mul(ca(36, 4), ca(28, 4), 1.0 / N))
                Vd(lambda e: e.tensor_mul(ca(40, 4), ca(32, 4), ca(32, 4)))
                Vd(lambda e: e.tensor_sub(ca(36, 4), ca(36, 4), ca(40, 4)))
                vv = p.at("cV")
                S(lambda e: e.activation(ca(40, 4), ca(36, 4), AF.Ln, bias=epsc),
                  waits=[("cV", vv), ln_consts])
                S(lambda e: e.activation(ca(44, 4), ca(40, 4), AF.Exp, scale=halfc))
                Sflush()
                rs = p.at("cS")
                for tc in range(4):
                    usl = uf[par][:, tc * N:(tc + 1) * N]
                    xsl = xn[:, tc * N:(tc + 1) * N]
                    V(lambda e, usl=usl, xsl=xsl, tc=tc:
                      e.tensor_scalar(xsl, usl, ca(32 + tc), ca(44 + tc),
                                      ALU.subtract, ALU.mult),
                      waits=[("cS", rs) if tc == 0 else None,
                             ubase_guard if tc == 0 else None])
                for tc in range(4):
                    usl = uf[par][:, tc * N:(tc + 1) * N]
                    xsl = xn[:, tc * N:(tc + 1) * N]
                    bsl = ubase[:, tc * N:(tc + 1) * N]
                    V(lambda e, xsl=xsl, bsl=bsl: e.tensor_mul(bsl, dbc[:, :], xsl),
                      waits=[small_ld if tc == 0 else None])
                    V(lambda e, usl=usl, bsl=bsl: e.tensor_add(bsl, bsl, usl))
                Vflush()
                xn_done = p.at("cV")
                dma("sync", ub_scr[m], ubase[:, :], waits=[("cV", xn_done)])
                ubase_guard = ("sSy", p.cnt["sSy"])

                for dc in range(8):
                    bank = dc % 2
                    for tc in range(4):
                        src = xn[:, tc * N + dc * 128: tc * N + (dc + 1) * 128]
                        dst = ps_t[bank][:, tc * 128:(tc + 1) * 128]
                        p.emit("tensor",
                               lambda e, src=src, dst=dst: e.transpose(dst, src, idt[:, :]),
                               waits=([("cV", xn_done), ps_t_guard[bank], small_ld]
                                      if tc == 0 else []),
                               inc="cT" if tc == 3 else None)
                    tv = p.at("cT")
                    dsl = xnt[:, dc * TT:(dc + 1) * TT]
                    V(lambda e, dsl=dsl, bank=bank: e.tensor_copy(dsl, ps_t[bank][:, :]),
                      waits=[("cT", tv), xnt_guard if dc == 0 else None])
                    p.mark_ppr()
                    ps_t_guard[bank] = p.at("cV")
                    p.emit("tensor", None, waits=[ps_t_guard[bank]])
                Vflush()
                xnt_done = p.at("cV")
                dma("sync", xnt_scr[m], xnt[:, :], waits=[("cS", xnt_done)])
                xnt_dma = ("sSy", p.cnt["sSy"])
                xn_guard = ("cT", p.cnt["cT"])

                for pc in range(4):
                    for half, fc in ((0, pc), (1, 4 + pc)):
                        ps = ps_b[half]
                        for dc in range(8):
                            lhs = bdev[:, dc * 1024 + fc * 128: dc * 1024 + (fc + 1) * 128]
                            rhs = xnt[:, dc * TT:(dc + 1) * TT]
                            p.emit("tensor",
                                   lambda e, ps=ps, lhs=lhs, rhs=rhs, dc=dc:
                                   e.matmul(ps[:, :], lhs, rhs, start=(dc == 0), stop=(dc == 7)),
                                   waits=([("cV", xnt_done), ps_b_guard[half],
                                           bdev_ready if (m == 0 and pc == 0) else None]
                                          if dc == 0 else []),
                                   inc="cT" if dc == 7 else None)
                    bv = p.at("cT")
                    csl = tabc[:, pc * 512:(pc + 1) * 512]
                    ssl = tabs[:, pc * 512:(pc + 1) * 512]
                    V(lambda e, csl=csl: e.tensor_mul(tmpa[:, :], csl, ps_b[0][:, :]),
                      waits=[("cT", bv), ("cV", tables_ready) if (m == 0 and pc == 0) else None])
                    V(lambda e, ssl=ssl: e.tensor_mul(tmpb[:, :], ssl, ps_b[1][:, :]))
                    V(lambda e: e.tensor_add(bp[:, 0:TT], tmpa[:, :], tmpb[:, :]))
                    V(lambda e, csl=csl: e.tensor_mul(tmpa[:, :], csl, ps_b[1][:, :]))
                    V(lambda e, ssl=ssl: e.tensor_mul(tmpb[:, :], ssl, ps_b[0][:, :]))
                    V(lambda e: e.tensor_sub(bp[:, TT:2 * TT], tmpa[:, :], tmpb[:, :]))
                    p.mark_ppr()
                    pr = p.at("cV")
                    p.emit("tensor", None, waits=[("cV", pr)])
                    ps_b_guard[0] = ("cV", pr)
                    ps_b_guard[1] = ("cV", pr)
                    xre = xf[par][:, pc * 1024: pc * 1024 + TT]
                    xim = xf[par][:, pc * 1024 + TT: (pc + 1) * 1024]
                    V(lambda e, xre=xre, pc=pc: e.tensor_tensor_scan(
                        xre, ones5[:, :], bp[:, 0:TT], icol[:, pc:pc + 1], ALU.mult, ALU.add),
                      waits=[xb_guard[par] if pc == 0 else None])
                    V(lambda e, xim=xim, pc=pc: e.tensor_tensor_scan(
                        xim, ones5[:, :], bp[:, TT:2 * TT], icol[:, 4 + pc:5 + pc],
                        ALU.mult, ALU.add))
                    xfs = xf[par][:, pc * 1024:(pc + 1) * 1024]
                    dma("sync", x_scr[m, pc], xfs, waits=[p.at("cV")])
                xb_guard[par] = ("sSy", p.cnt["sSy"])

                xv3 = xf[par].rearrange("p (a q) -> p a q", q=1024)
                xre_l = xv3[:, :, TT - 1:TT]
                xim_l = xv3[:, :, 1023:1024]
                i3r = icol[:, 0:4].rearrange("p (a b) -> p a b", b=1)
                i3i = icol[:, 4:8].rearrange("p (a b) -> p a b", b=1)
                l3r = l512[:, 0:4].rearrange("p (a b) -> p a b", b=1)
                l3i = l512[:, 4:8].rearrange("p (a b) -> p a b", b=1)
                t3a = ca(48, 4).rearrange("p (a b) -> p a b", b=1)
                t3b = ca(52, 4).rearrange("p (a b) -> p a b", b=1)
                t3c = ca(56, 4).rearrange("p (a b) -> p a b", b=1)
                Vd(lambda e, a=t3a, b=l3r, c=xre_l: e.tensor_mul(a, b, c))
                Vd(lambda e, a=t3b, b=l3i, c=xim_l: e.tensor_mul(a, b, c))
                Vd(lambda e, a=t3c, b=t3a, c=t3b: e.tensor_sub(a, b, c))
                Vd(lambda e, a=t3a, b=l3i, c=xre_l: e.tensor_mul(a, b, c))
                Vd(lambda e, a=t3b, b=l3r, c=xim_l: e.tensor_mul(a, b, c))
                Vd(lambda e, a=i3i, b=t3a, c=t3b: e.tensor_add(a, b, c))
                Vd(lambda e, a=i3r, b=t3c: e.tensor_copy(a, b))

                for tc in range(4):
                    for dc in range(8):
                        lhs = xnt[:, dc * TT + tc * 128: dc * TT + (tc + 1) * 128]
                        rhs = gwt[:, dc * 8:(dc + 1) * 8]
                        p.emit("tensor",
                               lambda e, lhs=lhs, rhs=rhs, dc=dc, tc=tc:
                               e.matmul(ps_l[:, tc * 8:(tc + 1) * 8], lhs, rhs,
                                        start=(dc == 0), stop=(dc == 7)),
                               waits=([("cV", xnt_done), ps_l_guard,
                                       ("cV", gwt_ready) if m == 0 else None]
                                      if (dc == 0 and tc == 0) else []),
                               inc="cT" if dc == 7 else None)
                lv = p.at("cT")

                def rb(tc, off, w=1):
                    b = 80 + 44 * tc
                    return colA[:, b + off:b + off + w]

                def step4(fn, waits=()):
                    for tc in range(4):
                        V((lambda e, tc=tc, fn=fn: fn(e, tc)),
                          waits=waits if tc == 0 else ())

                step4(lambda e, tc: e.reduce_max(rb(tc, 40), ps_l[:, tc * 8:(tc + 1) * 8], axis=AX),
                      waits=[("cT", lv)])
                step4(lambda e, tc: e.tensor_scalar(rb(tc, 0, 8), ps_l[:, tc * 8:(tc + 1) * 8],
                                                    rb(tc, 40), None, ALU.subtract))
                p.mark_ppr()
                ev = p.at("cV")
                ps_l_guard = ("cV", ev)
                for tc in range(4):
                    S(lambda e, tc=tc: e.activation(rb(tc, 0, 8), rb(tc, 0, 8), AF.Exp),
                      waits=[("cV", ev)] if tc == 0 else ())
                xv = p.at("cS")
                step4(lambda e, tc: e.reduce_sum(rb(tc, 40), rb(tc, 0, 8), axis=AX),
                      waits=[("cS", xv)])
                step4(lambda e, tc: e.reciprocal(rb(tc, 41), rb(tc, 40)))
                step4(lambda e, tc: e.tensor_scalar(rb(tc, 0, 8), rb(tc, 0, 8), rb(tc, 41), None, ALU.mult))
                step4(lambda e, tc: e.reduce_max(rb(tc, 40), rb(tc, 0, 8), axis=AX))
                step4(lambda e, tc: e.tensor_scalar(rb(tc, 8, 8), rb(tc, 0, 8), rb(tc, 40), None, ALU.is_ge))
                step4(lambda e, tc: e.tensor_add(rb(tc, 24 + 1, 7), rb(tc, 8 + 1, 7), rb(tc, 8, 7)))
                step4(lambda e, tc: e.tensor_copy(rb(tc, 24, 1), rb(tc, 8, 1)))
                step4(lambda e, tc: e.tensor_add(rb(tc, 16 + 2, 6), rb(tc, 24 + 2, 6), rb(tc, 24, 6)))
                step4(lambda e, tc: e.tensor_copy(rb(tc, 16, 2), rb(tc, 24, 2)))
                step4(lambda e, tc: e.tensor_add(rb(tc, 24 + 4, 4), rb(tc, 16 + 4, 4), rb(tc, 16, 4)))
                step4(lambda e, tc: e.tensor_copy(rb(tc, 24, 4), rb(tc, 16, 4)))
                step4(lambda e, tc: e.tensor_scalar(rb(tc, 16, 8), rb(tc, 24, 8), 1.0, None, ALU.is_equal))
                step4(lambda e, tc: e.tensor_mul(rb(tc, 32, 8), rb(tc, 8, 8), rb(tc, 16, 8)))
                step4(lambda e, tc: e.tensor_scalar_mul(rb(tc, 16, 8), rb(tc, 32, 8), -1.0e9))
                step4(lambda e, tc: e.tensor_add(rb(tc, 16, 8), rb(tc, 16, 8), rb(tc, 0, 8)))
                step4(lambda e, tc: e.reduce_max(rb(tc, 42), rb(tc, 16, 8), axis=AX))
                step4(lambda e, tc: e.tensor_scalar(rb(tc, 8, 8), rb(tc, 16, 8), rb(tc, 42), None, ALU.is_ge))
                step4(lambda e, tc: e.tensor_add(rb(tc, 24 + 1, 7), rb(tc, 8 + 1, 7), rb(tc, 8, 7)))
                step4(lambda e, tc: e.tensor_copy(rb(tc, 24, 1), rb(tc, 8, 1)))
                step4(lambda e, tc: e.tensor_add(rb(tc, 16 + 2, 6), rb(tc, 24 + 2, 6), rb(tc, 24, 6)))
                step4(lambda e, tc: e.tensor_copy(rb(tc, 16, 2), rb(tc, 24, 2)))
                step4(lambda e, tc: e.tensor_add(rb(tc, 24 + 4, 4), rb(tc, 16 + 4, 4), rb(tc, 16, 4)))
                step4(lambda e, tc: e.tensor_copy(rb(tc, 24, 4), rb(tc, 16, 4)))
                step4(lambda e, tc: e.tensor_scalar(rb(tc, 16, 8), rb(tc, 24, 8), 1.0, None, ALU.is_equal))
                step4(lambda e, tc: e.tensor_mul(rb(tc, 8, 8), rb(tc, 8, 8), rb(tc, 16, 8)))
                step4(lambda e, tc: e.reduce_max(rb(tc, 40), rb(tc, 0, 8), axis=AX))
                step4(lambda e, tc: e.tensor_add(rb(tc, 43), rb(tc, 40), rb(tc, 42)))
                step4(lambda e, tc: e.tensor_scalar_add(rb(tc, 43), rb(tc, 43), 1.0e-8))
                step4(lambda e, tc: e.reciprocal(rb(tc, 41), rb(tc, 43)))
                step4(lambda e, tc: e.tensor_scalar(rb(tc, 32, 8), rb(tc, 32, 8), rb(tc, 40), None, ALU.mult))
                step4(lambda e, tc: e.tensor_scalar(rb(tc, 8, 8), rb(tc, 8, 8), rb(tc, 42), None, ALU.mult))
                step4(lambda e, tc: e.tensor_add(rb(tc, 32, 8), rb(tc, 32, 8), rb(tc, 8, 8)))
                step4(lambda e, tc, m=m: e.tensor_scalar(
                    cwall[:, (m * 4 + tc) * 8:(m * 4 + tc + 1) * 8],
                    rb(tc, 32, 8), rb(tc, 41), None, ALU.mult))
                uf_guard[par] = p.at("cV")
                xnt_guard = [("cT", p.cnt["cT"]), xnt_dma]
            cw_ready = p.at("cV")

            if not lite:
                emit_pret(8)
            lastpar = (NTILE - 1) % 2
            for pc in range(4):
                V(lambda e, pc=pc: e.tensor_copy(ccbuf[:, pc:pc + 1],
                                                 xf[lastpar][:, pc * 1024 + TT - 1: pc * 1024 + TT]))
                V(lambda e, pc=pc: e.tensor_copy(ccbuf[:, 4 + pc:5 + pc],
                                                 xf[lastpar][:, (pc + 1) * 1024 - 1:(pc + 1) * 1024]))
            Vflush()
            ccp = p.at("cV")
            dma("gpsimd", cc_in[:, :], ccbuf[:, :], "sGp", waits=[("cV", ccp)])
            ccw = p.at("sGp")
            p.emit("gpsimd",
                   lambda e: e.collective_compute(
                       "AllGather", ALU.bypass,
                       replica_groups=[[0, 1, 2, 3], [4, 5, 6, 7]],
                       ins=[cc_in[:, :]], outs=[cc_out[:, :, :]]),
                   waits=[("sGp", ccw)], inc="sCC", inc_n=1)
            for r in range(4):
                dma("gpsimd", agt[r][:, :], cc_out[r], "sGp", waits=["sCC"])
            agl = p.at("sGp")

            Vd(lambda e: e.memset(icol[:, :], 0.0), waits=[("sGp", agl)])
            zd = ca(0, 8)
            t8 = ca(16, 8)
            for d in range(3):
                Vd(lambda e, zd=zd: e.memset(zd, 0.0))
                for r in range(4):
                    sc = selb[:, d * 4 + r:d * 4 + r + 1]
                    Vd(lambda e, r=r, sc=sc, t8=t8:
                      e.tensor_scalar(t8, agt[r][:, :], sc, None, ALU.mult))
                    Vd(lambda e, zd=zd, t8=t8: e.tensor_add(zd, zd, t8))
                for pc in range(4):
                    pre = pdl[:, d * 8 + pc:d * 8 + pc + 1]
                    pim = pdl[:, d * 8 + 4 + pc:d * 8 + 5 + pc]
                    zre = zd[:, pc:pc + 1]
                    zim = zd[:, 4 + pc:5 + pc]
                    t0, t1 = ca(32), ca(33)
                    Vd(lambda e, pre=pre, zre=zre, t0=t0: e.tensor_mul(t0, pre, zre))
                    Vd(lambda e, pim=pim, zim=zim, t1=t1: e.tensor_mul(t1, pim, zim))
                    Vd(lambda e, t0=t0, t1=t1: e.tensor_sub(ca(34), t0, t1))
                    Vd(lambda e, pc=pc: e.tensor_add(icol[:, pc:pc + 1], icol[:, pc:pc + 1], ca(34)))
                    Vd(lambda e, pim=pim, zre=zre, t0=t0: e.tensor_mul(t0, pim, zre))
                    Vd(lambda e, pre=pre, zim=zim, t1=t1: e.tensor_mul(t1, pre, zim))
                    Vd(lambda e, t0=t0, t1=t1: e.tensor_add(ca(34), t0, t1))
                    Vd(lambda e, pc=pc: e.tensor_add(icol[:, 4 + pc:5 + pc],
                                                    icol[:, 4 + pc:5 + pc], ca(34)))
            Vd(lambda e: e.tensor_copy(ccol[0][:, :], icol[:, :]))
            i0_ready = p.at("cV")

            if dbg:
                for nm, src in [("dbg_bdev", bdev), ("dbg_cdev", cdev),
                                ("dbg_tabc", tabc), ("dbg_tabs", tabs),
                                ("dbg_lam", lam), ("dbg_l512", l512),
                                ("dbg_icol", icol), ("dbg_gwt", gwt),
                                ("dbg_cw", cwall), ("dbg_xnt", xnt),
                                ("dbg_xn", xn), ("dbg_ub", ubase),
                                ("dbg_cols", dbg_cols_sb),
                                ("dbg_xf", xf[lastpar])]:
                    dma("gpsimd", dbg_out[nm][:, :], src[:, :], "sGp",
                        waits=[("cV", i0_ready)])
            p.flush()

        sBD.close()
        fenceC = p.snapshot()
        if lite:
            p.emit("sync", None, waits=fenceC)
            p.flush()
            return nc, dbg_out
        sC = contextlib.ExitStack()
        with sC:
            SBc = lambda name, shape, dt: sC.enter_context(nc.sbuf_tensor(name, shape, dt))
            ubl = [SBc(f"ubl{i}", [128, 4 * N], F32) for i in range(2)]
            xntl = [SBc(f"xntl{i}", [128, 8 * TT], BF16) for i in range(2)]
            xl = [SBc("xl0", [128, 4096], F32)]
            ghat = SBc("ghat", [128, 8 * TT], BF16)
            yacc = [SBc(f"yacc{i}", [128, 4 * N], F32) for i in range(2)]
            zt = SBc("zt", [128, 8 * TT], BF16)
            sig = [SBc(f"sig{i}", [128, TT], BF16) for i in range(2)]
            gsb = [SBc(f"gsb{i}", [128, TT], F32) for i in range(2)]
            upsb = [SBc(f"upsb{i}", [128, TT], F32) for i in range(2)]
            wring = [SBc(f"wring{i}", [128, 4096], BF16) for i in range(4)]
            pr1 = SBc("pr1", [128, TT], F32)
            pr2 = SBc("pr2", [128, TT], F32)
            tmc = SBc("tmc", [128, TT], F32)
            tmd = SBc("tmd", [128, TT], F32)
            ps_y = [sC.enter_context(nc.psum_tensor(f"ps_y{i}", [128, 512], F32)) for i in range(2)]
            ps_gu = [sC.enter_context(nc.psum_tensor(f"ps_gu{i}", [128, TT], F32)) for i in range(4)]
            ps_d = [sC.enter_context(nc.psum_tensor(f"ps_d{i}", [128, 512], F32)) for i in range(2)]
            ps_y_guard = [None, None]
            ps_gu_guard = [None, None, None, None]
            ps_d_guard = [None, None]
            ubl_guard = [None, None]
            xntl_guard = [None, None]
            xl_guard = [None]
            yacc_guard = [None, None]
            sig_guard = [None, None]
            upsb_guard = [None, None]
            gsb_guard = [None, None]
            zt_guard = None
            ghat_guard = None

            p.fence(fenceC)

            porder = [(m, ei, pn) for m in range(NTILE) for ei in range(NEXP)
                      for pn in range(6)]
            ring_free = [None] * 4
            piece_mark = {}
            next_load = 0

            def ensure(upto):
                nonlocal next_load
                while next_load <= min(upto + 3, len(porder) - 1):
                    m_, e_, pn_ = porder[next_load]
                    slot = next_load % 4
                    off, w = PIECE_OFF[pn_], PIECE_W[pn_]
                    xtra = [("cT", p.cnt["cT"])] if serw else []
                    dma("sync", wring[slot][:, 0:w], wdev[e_, :, off:off + w],
                        waits=[ring_free[slot], wdev_done[e_]] + xtra)
                    piece_mark[(m_, e_, pn_)] = ("sSy", p.cnt["sSy"])
                    next_load += 1

            def pidx(m, ei, pn):
                return (m * NEXP + ei) * 6 + pn

            loads = [None, None]

            def tile_loads(m, par):
                dma("sync", ubl[par][:, :], ub_scr[m], waits=[ubl_guard[par]])
                r1 = ("sSy", p.cnt["sSy"])
                dma("sync", xntl[par][:, :], xnt_scr[m], waits=[xntl_guard[par]])
                r2 = ("sSy", p.cnt["sSy"])
                return r1, r2

            def xl_load(m):
                dma("sync", xl[0][:, :].rearrange("p (a d) -> p a d", a=4),
                    x_scr[m].rearrange("a p d -> p a d"), waits=[xl_guard[0]])
                return ("sSy", p.cnt["sSy"])

            loads[0] = tile_loads(0, 0)
            xr_next = xl_load(0)
            ensure(pidx(0, 0, 3))

            for m in range(NTILE):
                if serc:
                    p.fence(p.snapshot())
                par = m % 2
                ubr, xntr = loads[par]
                xr = xr_next
                if m + 1 < NTILE:
                    loads[(m + 1) % 2] = tile_loads(m + 1, (m + 1) % 2)

                cpar = m % 2
                for pc in range(4):
                    csl = tabc[:, pc * 512:(pc + 1) * 512]
                    ssl = tabs[:, pc * 512:(pc + 1) * 512]
                    xre = xl[0][:, pc * 1024: pc * 1024 + TT]
                    xim = xl[0][:, pc * 1024 + TT:(pc + 1) * 1024]
                    cre = ccol[cpar][:, pc:pc + 1]
                    cim = ccol[cpar][:, 4 + pc:5 + pc]
                    gre = ghat[:, pc * TT:(pc + 1) * TT]
                    gim = ghat[:, (4 + pc) * TT:(5 + pc) * TT]
                    V(lambda e, xre=xre, cre=cre: e.tensor_scalar(pr1[:, :], xre, cre, None, ALU.add),
                      waits=[xr, ("cV", i0_ready) if m == 0 else None,
                             ghat_guard if pc == 0 else None])
                    V(lambda e, xim=xim, cim=cim: e.tensor_scalar(pr2[:, :], xim, cim, None, ALU.add))
                    V(lambda e, csl=csl: e.tensor_mul(tmc[:, :], csl, pr1[:, :]))
                    V(lambda e, ssl=ssl: e.tensor_mul(tmd[:, :], ssl, pr2[:, :]))
                    V(lambda e, gre=gre: e.tensor_sub(gre, tmc[:, :], tmd[:, :]))
                    V(lambda e, ssl=ssl: e.tensor_mul(tmc[:, :], ssl, pr1[:, :]))
                    V(lambda e, csl=csl: e.tensor_mul(tmd[:, :], csl, pr2[:, :]))
                    V(lambda e, gim=gim: e.tensor_add(gim, tmc[:, :], tmd[:, :]))
                if m + 1 < NTILE:
                    npar = (m + 1) % 2
                    cr4 = ccol[cpar][:, 0:4]
                    ci4 = ccol[cpar][:, 4:8]
                    Vd(lambda e, cr4=cr4: e.tensor_mul(ca(48, 4), l512[:, 0:4], cr4))
                    Vd(lambda e, ci4=ci4: e.tensor_mul(ca(52, 4), l512[:, 4:8], ci4))
                    Vd(lambda e: e.tensor_sub(ca(56, 4), ca(48, 4), ca(52, 4)))
                    Vd(lambda e, cr4=cr4: e.tensor_mul(ca(48, 4), l512[:, 4:8], cr4))
                    Vd(lambda e, ci4=ci4: e.tensor_mul(ca(52, 4), l512[:, 0:4], ci4))
                    Vd(lambda e, npar=npar: e.tensor_add(ccol[npar][:, 4:8], ca(48, 4), ca(52, 4)))
                    Vd(lambda e, npar=npar: e.tensor_copy(ccol[npar][:, 0:4], ca(56, 4)))
                Vflush()
                ghat_done = p.at("cV")
                xl_guard[0] = ("cV", ghat_done)
                if m + 1 < NTILE:
                    xr_next = xl_load(m + 1)

                for tc in range(4):
                    for dh in range(2):
                        bank = (tc * 2 + dh) % 2
                        ps = ps_y[bank]
                        for fc in range(8):
                            lhs = ghat[:, fc * TT + tc * 128: fc * TT + (tc + 1) * 128]
                            rhs = cdev[:, fc * 1024 + dh * 512: fc * 1024 + (dh + 1) * 512]
                            p.emit("tensor",
                                   lambda e, ps=ps, lhs=lhs, rhs=rhs, fc=fc:
                                   e.matmul(ps[:, :], lhs, rhs, start=(fc == 0), stop=(fc == 7)),
                                   waits=([("cV", ghat_done), ps_y_guard[bank],
                                           cdev_ready if m == 0 else None]
                                          if fc == 0 else []),
                                   inc="cT" if fc == 7 else None)
                        ym = p.at("cT")
                        ysl = yacc[par][:, tc * N + dh * 512: tc * N + (dh + 1) * 512]
                        usl = ubl[par][:, tc * N + dh * 512: tc * N + (dh + 1) * 512]
                        V(lambda e, ysl=ysl, usl=usl, ps=ps: e.tensor_add(ysl, usl, ps[:, :]),
                          waits=[("cT", ym), ubr, yacc_guard[par]])
                        p.mark_ppr()
                        ps_y_guard[bank] = p.at("cV")
                        p.emit("tensor", None, waits=[p.at("cV")])
                ghat_guard = ("cT", p.cnt["cT"])

                for ei in range(NEXP):
                    ensure(pidx(m, ei, 5))
                    s0_ = (pidx(m, ei, 0)) % 4
                    s1_ = (pidx(m, ei, 1)) % 4
                    s2_ = (pidx(m, ei, 2)) % 4
                    s3_ = (pidx(m, ei, 3)) % 4
                    s4_ = (pidx(m, ei, 4)) % 4
                    s5_ = (pidx(m, ei, 5)) % 4
                    for fc8 in range(8):
                        fw = 128 if fc8 < 7 else 48
                        foff = fc8 * 128
                        bank = fc8 % 2
                        ps = ps_gu[bank]
                        ps2 = ps_gu[2 + fc8 % 2]
                        for dc in range(8):
                            slot = s0_ if dc < 4 else s1_
                            lhs = wring[slot][:, (dc % 4) * FIP + foff:(dc % 4) * FIP + foff + fw]
                            rhs = xntl[par][:, dc * TT:(dc + 1) * TT]
                            p.emit("tensor",
                                   lambda e, ps=ps, lhs=lhs, rhs=rhs, dc=dc, fw=fw:
                                   e.matmul(ps[0:fw, :], lhs, rhs, start=(dc == 0), stop=(dc == 7)),
                                   waits=([piece_mark[(m, ei, 0)], piece_mark[(m, ei, 1)],
                                           ps_gu_guard[bank], xntr] if dc == 0 else []),
                                   inc="cT" if dc == 7 else None)
                        gv = p.at("cT")
                        sp = fc8 % 2
                        V(lambda e, ps=ps, sp=sp, fw=fw:
                          e.tensor_copy(gsb[sp][0:fw, :], ps[0:fw, :]),
                          waits=[("cT", gv), gsb_guard[sp]])
                        p.mark_ppr()
                        ps_gu_guard[bank] = p.at("cV")
                        gev = p.at("cV")
                        S(lambda e, sp=sp, fw=fw:
                          e.activation(sig[sp][0:fw, :], gsb[sp][0:fw, :], AF.Silu),
                          waits=[("cV", gev), sig_guard[sp]])
                        gsb_guard[sp] = ("cS", p.cnt["cS"])
                        sig_mark_last = ("cS", p.cnt["cS"])
                        for dc in range(8):
                            slot = s2_ if dc < 4 else s3_
                            lhs = wring[slot][:, (dc % 4) * FIP + foff:(dc % 4) * FIP + foff + fw]
                            rhs = xntl[par][:, dc * TT:(dc + 1) * TT]
                            p.emit("tensor",
                                   lambda e, ps2=ps2, lhs=lhs, rhs=rhs, dc=dc, fw=fw:
                                   e.matmul(ps2[0:fw, :], lhs, rhs, start=(dc == 0), stop=(dc == 7)),
                                   waits=([piece_mark[(m, ei, 2)], piece_mark[(m, ei, 3)],
                                           ps_gu_guard[2 + fc8 % 2]] if dc == 0 else []),
                                   inc="cT" if dc == 7 else None)
                        uv = p.at("cT")
                        V(lambda e, ps2=ps2, sp=sp, fw=fw:
                          e.tensor_copy(upsb[sp][0:fw, :], ps2[0:fw, :]),
                          waits=[("cT", uv), upsb_guard[sp]])
                        p.mark_ppr()
                        upev = p.at("cV")
                        ps_gu_guard[2 + fc8 % 2] = ("cV", upev)
                        zsl = zt[0:fw, fc8 * TT:fc8 * TT + TT]
                        V(lambda e, zsl=zsl, sp=sp, fw=fw:
                          e.tensor_mul(zsl, sig[sp][0:fw, :], upsb[sp][0:fw, :]),
                          waits=[sig_mark_last,
                                 zt_guard if fc8 == 0 else None])
                        zv = p.at("cV")
                        sig_guard[sp] = ("cV", zv)
                        upsb_guard[sp] = ("cV", zv)
                        p.emit("tensor", None, waits=[p.at("cV")])
                        if serf in ("1", "A"):
                            p.emit("tensor", None, waits=[p.at("cV"), ("cS", p.cnt["cS"])])
                        if serf == "D":
                            p.emit("tensor", None, waits=[("cS", p.cnt["cS"])])
                        if serf == "E":
                            p.emit("tensor", None, waits=[p.at("cV")])
                        if serf == "1":
                            p.emit("scalar", None, waits=[p.at("cV"), ("cT", p.cnt["cT"])])
                            p.emit("vector", None, waits=[("cS", p.cnt["cS"]), ("cT", p.cnt["cT"])])
                        if serf == "B":
                            p.emit("vector", None, waits=[("cS", p.cnt["cS"]), ("cT", p.cnt["cT"])])
                        if serf == "C":
                            p.emit("scalar", None, waits=[p.at("cV"), ("cT", p.cnt["cT"])])
                    Vflush()
                    zdone = p.at("cV")
                    gu_mm = ("cT", p.cnt["cT"])
                    for sl in (s0_, s1_, s2_, s3_):
                        ring_free[sl] = gu_mm
                    for tc in range(4):
                        for dh in range(2):
                            bank = (tc * 2 + dh) % 2
                            ps = ps_d[bank]
                            for ic in range(8):
                                fw = 128 if ic < 7 else 48
                                slot = s4_ if ic < 4 else s5_
                                lhs = zt[0:fw, ic * TT + tc * 128: ic * TT + (tc + 1) * 128]
                                rhs = wring[slot][0:fw, (ic % 4) * 1024 + dh * 512:
                                                  (ic % 4) * 1024 + (dh + 1) * 512]
                                p.emit("tensor",
                                       lambda e, ps=ps, lhs=lhs, rhs=rhs, ic=ic:
                                       e.matmul(ps[:, :], lhs, rhs, start=(ic == 0), stop=(ic == 7)),
                                       waits=([("cV", zdone), piece_mark[(m, ei, 4)],
                                               piece_mark[(m, ei, 5)], ps_d_guard[bank]]
                                              if ic == 0 else []),
                                       inc="cT" if ic == 7 else None)
                            dv = p.at("cT")
                            ysl = yacc[par][:, tc * N + dh * 512: tc * N + (dh + 1) * 512]
                            if ei == 0:
                                V(lambda e, ysl=ysl, ps=ps: e.tensor_add(ysl, ysl, ps[:, :]),
                                  waits=[("cT", dv)])
                            else:
                                cwc = cwall[:, (m * 4 + tc) * 8 + (ei - 1):
                                            (m * 4 + tc) * 8 + ei]
                                V(lambda e, ysl=ysl, ps=ps, cwc=cwc:
                                  e.scalar_tensor_tensor(ysl, ps[:, :], cwc, ysl,
                                                         ALU.mult, ALU.add),
                                  waits=[("cT", dv), cw_ready if (m == 0 and ei == 1) else None])
                            p.mark_ppr()
                            ps_d_guard[bank] = p.at("cV")
                            p.emit("tensor", None, waits=[p.at("cV")])
                    zt_guard = ("cT", p.cnt["cT"])
                    ring_free[s4_] = zt_guard
                    ring_free[s5_] = zt_guard
                Vflush()
                ydone = p.at("cV")
                for tc in range(4):
                    def fo(e, m=m, tc=tc, par=par):
                        return e.dma_start(out=out4[m * TT + tc * 128: m * TT + (tc + 1) * 128, :],
                                           in_=yacc[par][:, tc * N:(tc + 1) * N])
                    p.emit("sync", fo, waits=[("cV", ydone)], inc="sOut", inc_n=16)
                yacc_guard[par] = ("sOut", p.cnt["sOut"])
                ubl_guard[par] = ("cV", ydone)
                xntl_guard[par] = ("cT", p.cnt["cT"])

            if dbg:
                dma("gpsimd", dbg_out["dbg_wdev"][0], wdev[0], "sGp",
                    waits=[wdev_done[0]])
                dma("gpsimd", dbg_out["dbg_wdev"][1], wdev[5], "sGp",
                    waits=[wdev_done[5]])
                fin = p.snapshot()
                dma("gpsimd", dbg_out["dbg_ghat"][:, :], ghat[:, :], "sGp", waits=fin)
                dma("gpsimd", dbg_out["dbg_zt"][:, :], zt[:, :], "sGp")
                dma("gpsimd", dbg_out["dbg_yacc"][:, :], yacc[(NTILE - 1) % 2][:, :], "sGp")
                dma("gpsimd", dbg_out["dbg_xscr3"][:, :, :], x_scr[3], "sGp")
                dma("gpsimd", dbg_out["dbg_ubscr3"][:, :], ub_scr[3], "sGp")
                dma("gpsimd", dbg_out["dbg_xntscr3"][:, :], xnt_scr[3], "sGp")
                p.emit("sync", None, waits=["sGp"])
            p.emit("sync", None, waits=[("sOut", p.cnt["sOut"])])
            p.flush()

    return nc, dbg_out


_CW_READY_HACK = None


def _per_core_inputs(inputs, consts, core):
    b, s = core // 4, core % 4
    u = np.ascontiguousarray(inputs["L0_attn_u"][b, s * TS:(s + 1) * TS, :])
    sel = np.zeros((3, 4), dtype=np.float32)
    for d in (1, 2, 3):
        r = s - d
        if r >= 0:
            sel[d - 1, r] = 1.0
    m = {
        "u4": u.astype(np.float32, copy=False),
        "ap_bc": np.ascontiguousarray(
            np.broadcast_to(inputs["L0_attn_a_params"][None, :], (128, 512))).astype(np.float32),
        "bw": inputs["L0_attn_B_w"],
        "cwm": inputs["L0_attn_C_w"],
        "dv_bc": np.ascontiguousarray(
            np.broadcast_to(inputs["L0_attn_D"][None, :], (128, N))).astype(np.float32),
        "gwt_in": inputs["L0_ffn_gate_weight"],
        "sgu": inputs["L0_ffn_shared_gate_up"],
        "sdn": inputs["L0_ffn_shared_down"],
        "egu": inputs["L0_ffn_experts_gate_up"],
        "edn": inputs["L0_ffn_experts_down"],
        "sel_bc": np.ascontiguousarray(
            np.broadcast_to(sel.reshape(1, 12), (128, 12))).astype(np.float32),
    }
    m.update(consts)
    return m


def kernel(**inputs):
    inputs = {k: np.asarray(v) for k, v in inputs.items()}
    consts = _host_constants()
    nc, _ = build_program(dbg=False)
    in_maps = [_per_core_inputs(inputs, consts, c) for c in range(8)]
    res = run_bass_kernel_spmd(nc, in_maps, core_ids=list(range(8)))
    out = np.empty((2, 16384, N), dtype=np.float32)
    for c in range(8):
        b, s = c // 4, c % 4
        out[b, s * TS:(s + 1) * TS, :] = res.results[c]["out4"]
    return out
